# version 02 (final)
# baseline (speedup 1.0000x reference)
import contextlib
import os
import numpy as np
import concourse.bass as bass
import concourse.mybir as mybir
from concourse.bass_utils import run_bass_kernel_spmd

F32 = mybir.dt.float32
BF16 = mybir.dt.bfloat16
AX = mybir.AxisListType.X
ALU = mybir.AluOpType
ACT = mybir.ActivationFunctionType

L, D, FF, MODN = 4096, 1024, 4096, 6144
NT = 32
TPS = 4
NSC = NT // TPS
EPS = 1e-5
ENGS = ("pe", "ve", "se", "sp", "gp")
SEMS = ("pe", "ve", "se", "sp", "gx", "gw", "gs")
GP, SP, VE, SE, PE = "gp", "sp", "ve", "se", "pe"


class SB:

    def __init__(self):
        self.items = {e: [] for e in ENGS}
        self.cnt = {m: 0 for m in SEMS}
        self.waited = {e: {} for e in ENGS}
        self.writer = {}
        self.readers = {}

    def _wait(self, eng, sm, wc, force=False):
        if (sm == eng and not force) or self.waited[eng].get(sm, 0) >= wc:
            return
        self.waited[eng][sm] = wc
        self.items[eng].append(("wait", sm, wc))

    def emit(self, eng, fns, reads=(), writes=(), dma=False, dsem=None,
             slow=False):
        if not isinstance(fns, (list, tuple)):
            fns = [fns]
        sm = (dsem or ("sp" if eng == "sp" else "gx")) if dma else eng
        evs = []
        for r in reads:
            ev = self.writer.get(r)
            if ev:
                evs.append(ev)
        for w in writes:
            ev = self.writer.get(w)
            if ev:
                evs.append(ev)
            evs.extend(self.readers.get(w, ()))
        need = {}
        for (we, wc, hard) in evs:
            if hard or we != eng:
                if wc > need.get(we, 0):
                    need[we] = wc
        for we in sorted(need):
            self._wait(eng, we, need[we], force=True)
        n = 16 if dma else 1
        self.cnt[sm] += n
        myev = (sm, self.cnt[sm], bool(slow))
        if eng in ("ve", "se", "pe") and not dma:
            if getattr(fns[-1], "_is_drain", False) is False:
                d = lambda e: e.drain()
                d._is_drain = True
                fns = list(fns) + [d]
        for i, fn in enumerate(fns):
            self.items[eng].append(("inst", fn, n if i == len(fns) - 1 else 0, sm))
        for r in reads:
            self.readers.setdefault(r, []).append(myev)
        for w in writes:
            self.writer[w] = myev
            self.readers[w] = []
        return myev

    def barrier(self):
        for e in ("pe", "ve", "se"):
            self.cnt[e] += 1
            self.items[e].append(("inst", (lambda eng: eng.drain()), 0, e))
            self.items[e].append(("inst", (lambda eng: eng.nop()), 1, e))
        tot = dict(self.cnt)
        for e in ENGS:
            for sm in SEMS:
                if sm != e:
                    self._wait(e, sm, tot[sm])
        self.writer.clear()
        self.readers.clear()

    def finalize(self, eng):
        for sm in SEMS:
            if sm != eng:
                self._wait(eng, sm, self.cnt[sm])
        if self.cnt.get(eng):
            self.items[eng].append(("wait", eng, self.cnt[eng]))


    def replay(self, block, sems):
        def runner(name):
            def run(eng):
                for it in self.items[name]:
                    if it[0] == "wait":
                        eng.wait_ge(sems[it[1]], it[2])
                    else:
                        inst = it[1](eng)
                        if it[2]:
                            inst.then_inc(sems[it[3]], it[2])
            return run
        block.tensor(runner("pe"))
        block.vector(runner("ve"))
        block.scalar(runner("se"))
        block.sync(runner("sp"))
        block.gpsimd(runner("gp"))


def build(tap=None):
    nc = bass.Bass("TRN2", target_bir_lowering=False)

    def din(name, shape):
        return nc.dram_tensor(name, shape, F32, kind="ExternalInput").ap()

    X = din("x", [L, D])
    WQ = din("wq_in", [D, D])
    WK = din("wk_in", [D, D])
    WV = din("wv_in", [D, D])
    WO = din("wo_in", [D, D])
    BO, GAM, BET = din("bo_in", [D]), din("gamma_in", [D]), din("beta_in", [D])
    W1, B1 = din("w1_in", [D, FF]), din("b1_in", [FF])
    W2, B2 = din("w2_in", [FF, D]), din("b2_in", [D])
    CC, WM = din("c_in", [D]), din("wmod_in", [D, MODN])
    WMP = din("wmodpc_in", [D, 4 * D])
    IDN = din("ident_in", [128, 128])
    Y = nc.dram_tensor("y", [L, D], F32, kind="ExternalOutput").ap()
    XNEW = nc.dram_tensor("xnew_scr", [L, D], F32, kind="Internal").ap()
    ROWSD = nc.dram_tensor("rows_scr", [1, 12 * D], BF16, kind="Internal").ap()
    JNK = nc.dram_tensor("jnk_scr", [128, 4096], F32, kind="Internal").ap()

    sb = SB()

    with contextlib.ExitStack() as outer:
        sems = {m: outer.enter_context(nc.semaphore(f"sem_{m}")) for m in SEMS}
        block = outer.enter_context(nc.Block())
        psA = outer.enter_context(nc.psum_tensor("psA", [128, 4096], F32))

        def sb_t(st, name, shape, dt, side=None):
            return st.enter_context(nc.sbuf_tensor(name, shape, dt, side=side))

        idn = sb_t(outer, "idn", [128, 128], F32)
        ones_bf = sb_t(outer, "ones_bf", [1, 128], BF16)
        c8 = sb_t(outer, "c8", [8, 128], F32)
        sil8f = sb_t(outer, "sil8f", [128, 128], BF16)
        silcF = sb_t(outer, "silcF", [128, 128], BF16)
        silcG = sb_t(outer, "silcG", [128, 8], BF16)
        idnb = sb_t(outer, "idnb", [128, 128], BF16)
        ld32 = sb_t(outer, "ld32", [32, 128], F32)
        bo_pc = sb_t(outer, "bo_pc", [128, 8], F32)
        b1_pc = sb_t(outer, "b1_pc", [128, 32], F32)
        b2_pc = sb_t(outer, "b2_pc", [128, 8], F32)
        scp = sb_t(outer, "scp", [128, 32], F32)
        scp_bf = sb_t(outer, "scp_bf", [128, 16], BF16)
        sawk = sb_t(outer, "sawk", [128, 8], F32)
        savw_row = sb_t(outer, "savw_row", [1, D], BF16)
        b1f_pc = sb_t(outer, "b1f_pc", [128, 32], F32)
        stat_rstd = sb_t(outer, "stat_rstd", [128, 4 * NT], F32)
        stat_nmr = sb_t(outer, "stat_nmr", [128, 4 * NT], F32)

        negmax = sb_t(outer, "negmaxk", [128, 32], F32)
        kpart = sb_t(outer, "kpart", [128, 64], F32)
        ksum = sb_t(outer, "ksum", [128, 8], F32)
        rk = sb_t(outer, "rk", [128, 8], F32)
        TA2 = sb_t(outer, "TA2", [128, 512], F32)
        KMX = sb_t(outer, "KMX", [128, 16], F32)
        epsb = sb_t(outer, "epsb", [128, 1], F32)

        XB = sb_t(outer, "XB", [128, 2 * D], BF16)
        HT = sb_t(outer, "HT", [128, 8192], BF16)
        XRs = sb_t(outer, "XRs", [128, 2 * D], F32)
        SC0 = sb_t(outer, "SC0", [128, D], F32)
        ctx = sb_t(outer, "ctx", [128, 8192], BF16, side="right")

        def mm(ps_sl, lhsT, rhs, start, stop):
            return lambda e, a=ps_sl, b=lhsT, c=rhs, s=start, t=stop: \
                e.matmul(a, b, c, start=s, stop=t)

        def bank(i, w=512):
            return psA[:, i * 512:i * 512 + w]

        def PSB(*banks):
            return [("psb", b) for b in banks]

        def pretouch(buf, key, cols):
            for c0 in range(0, cols, 4096):
                w = min(4096, cols - c0)
                sb.emit(GP, lambda e, b=buf, c0=c0, w=w: e.dma_start(
                    out=b[:, c0:c0 + w], in_=JNK[:, 0:w]),
                    writes=[(key,)], dma=True, dsem="gs")

        pretouch(silcG, "silc", 8)
        pretouch(scp_bf, "scpbf", 16)
        sb.emit(SP, lambda e: e.dma_start(out=idn[:], in_=IDN),
                writes=[("idn",)], dma=True)
        sb.emit(SP, lambda e: e.dma_start(
            out=c8[:], in_=CC.rearrange("(k p) -> k p", p=128)),
            writes=[("c8",)], dma=True)
        sb.emit(GP, lambda e: e.dma_start(out=ones_bf[:], in_=JNK[0:1, 0:128]),
                writes=[("ones",)], dma=True, dsem="gs")
        sb.emit(VE, lambda e: e.memset(ones_bf[:], 1.0), writes=[("ones",)])
        sb.emit(VE, lambda e: e.memset(epsb[:], EPS), writes=[("eps",)])

        def load_pc(dst, src_dram, n, key, po):
            sb.emit(SP, lambda e, n=n, s=src_dram: e.dma_start(
                out=ld32[0:n, :], in_=s.rearrange("(k p) -> k p", p=128)),
                writes=[("ld32",)], dma=True)
            sb.emit(PE, lambda e, n=n, po=po: e.transpose(
                psA[:, po:po + n], ld32[0:n, :], idn[0:n, 0:n]),
                reads=[("ld32",), ("idn",)], writes=PSB(7))
            sb.emit(VE, lambda e, d=dst, n=n, po=po: e.tensor_copy(d[:], psA[:, po:po + n]),
                    reads=PSB(7), writes=[(key,)])

        load_pc(bo_pc, BO, 8, "bo_pc", 3584)
        load_pc(b1_pc, B1, 32, "b1_pc", 3600)
        load_pc(b2_pc, B2, 8, "b2_pc", 3648)

        stA = contextlib.ExitStack()
        wkb = sb_t(stA, "wkb", [128, 8 * D], BF16)
        kbuf = sb_t(stA, "kbuf", [128, NT * D], BF16)
        st0 = contextlib.ExitStack()
        wmpp = sb_t(st0, "wmpp", [128, 4 * 3072], BF16)
        gax = sb_t(st0, "gax", [128, 16], F32)
        gsg = sb_t(st0, "gsg", [128, 16], F32)
        gac = sb_t(st0, "gac", [128, 16], F32)
        gpcg = sb_t(st0, "gpcg", [128, 8], F32)
        gpcb = sb_t(st0, "gpcb", [128, 8], F32)
        TSTK = sb_t(st0, "TSTK", [128, 128], BF16)
        TROWF = sb_t(st0, "TROWF", [128, 128], BF16)

        sb.emit(SE, lambda e: e.activation(sil8f[0:8, :], c8[:], ACT.Silu),
                reads=[("c8",)], writes=[("sil8",)])
        sb.emit(SP, lambda e: e.dma_start_transpose(silcF[:], sil8f[:]),
                reads=[("sil8",)], writes=[("silcf",)], dma=True)
        sb.emit(VE, lambda e: e.tensor_copy(silcG[:], silcF[:, 0:8]),
                reads=[("silcf",)], writes=[("silc",)])

        if tap == "silc":
            sb.emit(SP, lambda e: e.dma_start(out=Y[0:8, 0:128], in_=c8[:]),
                    reads=[("c8",)], dma=True)
            sb.emit(GP, lambda e: e.dma_start(out=Y[8:16, 0:128], in_=sil8f[0:8, :]),
                    reads=[("sil8",)], dma=True, dsem="gs")
            sb.emit(GP, lambda e: e.dma_start(out=Y[128:256, 0:8], in_=silcG[:]),
                    reads=[("silc",)], dma=True, dsem="gs")
            sb.finalize(GP)
            sb.finalize(SP)
            sb.replay(block, sems)
            st0.close()
            stA.close()
            return nc
        load_pc(gpcg, GAM, 8, "gpcg", 3696)
        load_pc(gpcb, BET, 8, "gpcb", 3712)
        for k in range(8):
            s0 = (k % 2) * 2
            for hh in range(2):
                sb.emit(GP, lambda e, k=k, sl=s0 + hh, hh=hh: e.dma_start(
                    out=wmpp[:, sl * 3072:(sl + 1) * 3072],
                    in_=WM[k * 128:(k + 1) * 128, hh * 3072:(hh + 1) * 3072]),
                    writes=[("wm", s0 + hh)], dma=True, dsem="gw")
            fns = []
            for v in range(6):
                sl = s0 + v // 3
                off = (v % 3) * 1024
                for cc in range(8):
                    fns.append(mm(
                        psA[:, v * 64 + cc * 8 + k:v * 64 + cc * 8 + k + 1],
                        wmpp[:, sl * 3072 + off + cc * 128:
                             sl * 3072 + off + cc * 128 + 128],
                        silcG[:, k:k + 1], True, True))
            sb.emit(PE, fns,
                    reads=[("wm", s0), ("wm", s0 + 1), ("silc",)], writes=PSB(0))
            if os.environ.get("P0BAR"):
                sb.barrier()
        if os.environ.get("P0BAR"):
            sb.barrier()
        sb.emit(PE, lambda e: e.matmul(psA[:, 384:512], wmpp[:, 0:128],
                                       wmpp[:, 0:128], start=True, stop=True),
                reads=[("wm", 0)], writes=PSB(0))
        sb.emit(SE, lambda e: e.copy(SC0[:, 0:384], psA[:, 0:384]),
                reads=PSB(0), writes=[("sc0",)])
        if tap == "sc0":
            sb.emit(GP, lambda e: e.dma_start(out=idnb[:], in_=IDN),
                    writes=[("idnb",)], dma=True, dsem="gs")
            fns = [mm(psA[:, 400 + j:401 + j], idnb[:], silcG[:, j:j + 1],
                      True, True) for j in range(8)]
            fns.append(mm(psA[:, 408:409], wmpp[:, 2 * 3072:2 * 3072 + 128],
                          idnb[:, 0:1], True, True))
            sb.emit(PE, fns, reads=[("idnb",), ("silc",), ("wm", 2)],
                    writes=PSB(0))
            sb.emit(SE, lambda e: e.copy(SC0[:, 384:512], psA[:, 384:512]),
                    reads=PSB(0), writes=[("sc0x",)])
            sb.emit(SP, lambda e: e.dma_start(out=Y[0:128, 0:384], in_=SC0[:, 0:384]),
                    reads=[("sc0",)], dma=True)
            sb.emit(SP, lambda e: e.dma_start(out=Y[128:256, 0:128],
                                              in_=SC0[:, 384:512]),
                    reads=[("sc0x",)], dma=True)
            sb.finalize(SP)
            sb.finalize(GP)
            sb.replay(block, sems)
            st0.close()
            stA.close()
            return nc
        def pc_reduce(dst, base, v, key):
            for cc in range(8):
                sb.emit(VE, lambda e, dst=dst, b=base, v=v, cc=cc: e.reduce_sum(
                    dst[:, b + cc:b + cc + 1],
                    SC0[:, v * 64 + cc * 8:v * 64 + (cc + 1) * 8], axis=AX),
                    reads=[("sc0",)], writes=[key], slow=True)

        pc_reduce(scp, 0, 0, ("scp",))
        pc_reduce(scp, 8, 1, ("scp",))
        pc_reduce(scp, 16, 3, ("scp",))
        pc_reduce(scp, 24, 4, ("scp",))
        pc_reduce(gax, 0, 2, ("gax",))
        pc_reduce(gax, 8, 5, ("gax",))
        sb.emit(VE, lambda e: e.tensor_copy(scp_bf[:, 0:8], scp[:, 0:8]),
                reads=[("scp",)], writes=[("scpbf",)])
        sb.emit(VE, lambda e: e.tensor_copy(scp_bf[:, 8:16], scp[:, 16:24]),
                reads=[("scp",)], writes=[("scpbf",)])
        sb.emit(VE, lambda e: e.tensor_scalar_add(scp[:, 8:16], scp[:, 8:16], 1.0),
                reads=[("scp",)], writes=[("scp",)])
        sb.emit(VE, lambda e: e.tensor_scalar_add(scp[:, 24:32], scp[:, 24:32], 1.0),
                reads=[("scp",)], writes=[("scp",)])
        sb.emit(SE, lambda e: e.activation(gsg[:], gax[:], ACT.Sign),
                reads=[("gax",)], writes=[("gsg",)])
        sb.emit(VE, lambda e: e.tensor_mul(gac[:], gax[:], gsg[:]),
                reads=[("gax",), ("gsg",)], writes=[("gac",)])
        sb.emit(VE, lambda e: e.tensor_scalar_max(gac[:], gac[:], 0.25),
                reads=[("gac",)], writes=[("gac",)])
        sb.emit(VE, lambda e: e.tensor_mul(gac[:], gac[:], gsg[:]),
                reads=[("gac",), ("gsg",)], writes=[("gac",)])
        sb.emit(VE, lambda e: e.tensor_copy(TSTK[:, 0:8], scp[:, 0:8]),
                reads=[("scp",)], writes=[("tstk",)])
        sb.emit(VE, lambda e: e.tensor_copy(TSTK[:, 8:16], scp[:, 8:16]),
                reads=[("scp",)], writes=[("tstk",)])
        sb.emit(VE, lambda e: e.tensor_mul(TSTK[:, 16:24], gac[:, 0:8], gpcb[:]),
                reads=[("gac",), ("gpcb",)], writes=[("tstk",)])
        sb.emit(VE, lambda e: e.tensor_mul(TSTK[:, 24:32], gac[:, 0:8], gpcg[:]),
                reads=[("gac",), ("gpcg",)], writes=[("tstk",)])
        sb.emit(VE, lambda e: e.tensor_copy(TSTK[:, 32:40], gac[:, 8:16]),
                reads=[("gac",)], writes=[("tstk",)])
        sb.emit(SP, lambda e: e.dma_start_transpose(TROWF[:], TSTK[:]),
                reads=[("tstk",)], writes=[("trow",)], dma=True)
        sb.emit(SP, lambda e: e.dma_start(
            out=ROWSD[0:1, 0:5 * D].rearrange("o (r p) -> (o r) p", p=128),
            in_=TROWF[0:40, :]), reads=[("trow",)], writes=[("rowsd",)], dma=True)
        if tap == "rowsd":
            sb.emit(GP, lambda e: e.dma_start(
                out=Y.rearrange("l d -> (l d)")[0:5 * D].rearrange("(o a) -> o a", o=1),
                in_=ROWSD[0:1, 0:5 * D]), reads=[("rowsd",)], dma=True)
            sb.finalize(GP)
            sb.finalize(SP)
            sb.replay(block, sems)
            st0.close()
            stA.close()
            return nc
        if tap == "scp":
            sb.emit(SP, lambda e: e.dma_start(out=Y[0:128, 0:32], in_=scp[:]),
                    reads=[("scp",)], dma=True)
            sb.emit(GP, lambda e: e.dma_start(out=Y[128:256, 0:16], in_=scp_bf[:]),
                    reads=[("scpbf",)], dma=True, dsem="gs")
            sb.finalize(SP)
            sb.finalize(GP)
            sb.replay(block, sems)
            st0.close()
            stA.close()
            return nc
        def ln_stats_xhat(t, xr, xkey, dst, dkey, rstd_ap, nmr_ap, rkey, nkey,
                          do_stats, TAx=None):
            b = 16 * t
            a0 = TAx[:, b:b + 1]
            a1 = TAx[:, b + 4:b + 5]
            a2 = TAx[:, b + 8:b + 9]
            sd = TAx[:, b + 12:b + 13]
            k0, k1, k2, k3 = ("ta", t, 0), ("ta", t, 1), ("ta", t, 2), ("ta", t, 3)
            if do_stats:
                sb.emit(VE, lambda e, xr=xr, a0=a0: e.reduce_sum(
                    a0, xr, axis=AX, negate=True), reads=[xkey], writes=[k0],
                    slow=True)
                sb.emit(SE, [lambda e, xr=xr, a1=a1: e.activation(
                    SC0[:], xr, ACT.Square, accum_out=a1),
                    lambda e: e.drain()],
                    reads=[xkey], writes=[("sc0",), k1], slow=True)
                sb.emit(VE, lambda e, a0=a0: e.tensor_scalar_mul(a0, a0, 1.0 / D),
                        reads=[k0], writes=[k0])
                sb.emit(VE, lambda e, a1=a1: e.tensor_scalar_mul(a1, a1, 1.0 / D),
                        reads=[k1], writes=[k1])
                sb.emit(VE, lambda e, a0=a0, a2=a2: e.tensor_mul(a2, a0, a0),
                        reads=[k0], writes=[k2])
                sb.emit(VE, [lambda e, a1=a1, a2=a2: e.tensor_sub(a1, a1, a2),
                             lambda e: e.drain()],
                        reads=[k1, k2], writes=[k1])
                sb.emit(SE, lambda e, a1=a1, sd=sd: e.activation(
                    sd, a1, ACT.Sqrt, bias=epsb[:, 0:1]),
                    reads=[k1], writes=[k3])
                sb.emit(VE, [lambda e, sd=sd, r=rstd_ap: e.reciprocal(r, sd),
                             lambda e: e.drain()],
                        reads=[k3], writes=[rkey])
                sb.emit(VE, [lambda e, a0=a0, r=rstd_ap, nm=nmr_ap: e.tensor_mul(nm, a0, r),
                             lambda e: e.drain()],
                        reads=[k0, rkey], writes=[nkey])
            sb.emit(SE, lambda e, xr=xr, d=dst, r=rstd_ap, nm=nmr_ap: e.activation(
                d, xr, ACT.Identity, bias=nm, scale=r),
                reads=[xkey, rkey, nkey], writes=[dkey])

        def emit_x_tile(t, xr_ap, xkey, src=X):
            sb.emit(GP, lambda e, t=t, d=xr_ap, src=src: e.dma_start(
                out=d, in_=src[t * 128:(t + 1) * 128, :]),
                writes=[xkey], dma=True)

        def emit_T(dst4096, dstkey, srcD, srckey, off):
            sb.emit(SP, lambda e, d=dst4096, s=srcD, o=off: e.dma_start_transpose(
                d.rearrange("p (c j) -> p c j", c=8)[:, :, o:o + 128],
                s.rearrange("p (c r) -> p c r", c=8)),
                reads=[srckey], writes=[dstkey], dma=True)

        sb.emit(GP, lambda e: e.dma_start(
            out=wkb[:].rearrange("p (c n) -> p c n", c=8),
            in_=WK.rearrange("(c p) n -> p c n", p=128)), writes=[("wk",)], dma=True, dsem="gw")
        for cc in range(8):
            fns = [mm(psA[:, 3104 + cc:3105 + cc],
                      wkb[:, k * 1024 + cc * 128:k * 1024 + cc * 128 + 128],
                      scp_bf[:, k:k + 1], k == 0, k == 7) for k in range(8)]
            sb.emit(PE, fns, reads=[("wk",), ("scpbf",)], writes=PSB(6))
        sb.emit(VE, lambda e: e.tensor_copy(sawk[:], psA[:, 3104:3112]),
                reads=PSB(6), writes=[("sawk",)])
        for k in range(8):
            sb.emit(VE, lambda e, k=k: e.tensor_scalar_mul(
                wkb[:, k * 1024:(k + 1) * 1024], wkb[:, k * 1024:(k + 1) * 1024],
                scp[:, 8 + k:9 + k]),
                reads=[("wk",), ("scp",)] + PSB(6), writes=[("wk",)])

        kview = kbuf[:].rearrange("p (t r) -> p t r", t=NT)
        for sc in range(NSC):
            hts = sc % 2
            for i in range(TPS):
                t = sc * TPS + i
                xs, xbs = t % 2, t % 2
                xr = XRs[:, xs * D:(xs + 1) * D]
                emit_x_tile(t, xr, ("xr", xs))
                ln_stats_xhat(t, xr, ("xr", xs),
                              XB[:, xbs * D:(xbs + 1) * D], ("xb", xbs),
                              stat_rstd[:, 4 * t:4 * t + 1], stat_nmr[:, 4 * t:4 * t + 1],
                              ("rstd", t), ("nmr", t), True, TAx=TA2)
                emit_T(HT[:, hts * 4096:(hts + 1) * 4096], ("hT", hts, i),
                       XB[:, xbs * D:(xbs + 1) * D], ("xb", xbs), i * 128)
            if tap == "st" and sc == 0:
                sb.emit(SP, lambda e: e.dma_start(
                    out=Y[0:128, 0:16], in_=stat_rstd[:, 0:16]),
                    reads=[("rstd", t) for t in range(4)], dma=True)
                sb.emit(SP, lambda e: e.dma_start(
                    out=Y[128:256, 0:16], in_=stat_nmr[:, 0:16]),
                    reads=[("nmr", t) for t in range(4)], dma=True)
                sb.finalize(SP)
                sb.finalize(GP)
                sb.replay(block, sems)
                st0.close()
                stA.close()
                return nc
            if tap == "ht" and sc == 0:
                sb.emit(GP, lambda e: e.dma_start(
                    out=Y[0:512, :].rearrange("(a p) d -> p a d", p=128),
                    in_=HT[:, 0:4096].rearrange("p (a d) -> p a d", a=4)),
                    reads=[("hT", 0, i) for i in range(TPS)], dma=True)
                sb.finalize(GP)
                sb.finalize(SP)
                sb.replay(block, sems)
                st0.close()
                stA.close()
                return nc
            for m in range(8):
                bk = m % 4
                fns = [mm(bank(bk),
                          wkb[:, c * 1024 + m * 128:c * 1024 + m * 128 + 128],
                          HT[:, hts * 4096 + c * 512:hts * 4096 + (c + 1) * 512],
                          c == 0, c == 7) for c in range(8)]
                sb.emit(PE, fns,
                        reads=[("wk",)] + [("hT", hts, i) for i in range(TPS)],
                        writes=PSB(bk))
                sb.emit(SE, lambda e, m=m, sc=sc, bk=bk: e.activation(
                    kview[:, sc * TPS:(sc + 1) * TPS, m * 128:(m + 1) * 128],
                    bank(bk), ACT.Identity, bias=sawk[:, m:m + 1]),
                    reads=PSB(bk) + [("sawk",)], writes=[("kT", m, sc)])
            if sc == 0:
                for m in range(8):
                    for i in range(TPS):
                        sb.emit(VE, lambda e, m=m, i=i: e.reduce_max(
                            KMX[:, 4 * i:4 * i + 1],
                            kbuf[:, i * 1024 + m * 128:i * 1024 + (m + 1) * 128],
                            axis=AX),
                            reads=[("kT", m, 0)], writes=[("kmx", i)], slow=True)
                    sb.emit(VE, lambda e: e.tensor_max(
                        KMX[:, 1:2], KMX[:, 0:1], KMX[:, 4:5]),
                        reads=[("kmx", 0), ("kmx", 1)], writes=[("kmx", 0)])
                    sb.emit(VE, lambda e: e.tensor_max(
                        KMX[:, 9:10], KMX[:, 8:9], KMX[:, 12:13]),
                        reads=[("kmx", 2), ("kmx", 3)], writes=[("kmx", 2)])
                    sb.emit(VE, lambda e: e.tensor_max(
                        KMX[:, 2:3], KMX[:, 1:2], KMX[:, 9:10]),
                        reads=[("kmx", 0), ("kmx", 2)], writes=[("kmx", 0)])
                    sb.emit(VE, lambda e, m=m: e.tensor_scalar_mul(
                        negmax[:, 4 * m:4 * m + 1], KMX[:, 2:3], -1.0),
                        reads=[("kmx", 0)], writes=[("ngm", m)])
            if tap != "kraw":
                for m in range(8):
                    sl = kview[:, sc * TPS:(sc + 1) * TPS, m * 128:(m + 1) * 128]
                    sb.emit(SE, lambda e, s=sl, m=m, sc=sc: e.activation(
                        s, s, ACT.Exp, bias=negmax[:, 4 * m:4 * m + 1],
                        accum_out=kpart[:, m * 8 + sc:m * 8 + sc + 1]),
                        reads=[("kT", m, sc), ("ngm", m)],
                        writes=[("kT", m, sc), ("kp", m, sc)])
        for m in range(8):
            sb.emit(VE, lambda e, m=m: e.reduce_sum(
                ksum[:, m:m + 1], kpart[:, m * 8:(m + 1) * 8], axis=AX),
                reads=[("kp", m, s) for s in range(NSC)], writes=[("ksum",)],
                slow=True)
        sb.emit(VE, lambda e: e.reciprocal(rk[:], ksum[:]),
                reads=[("ksum",)], writes=[("rk",)])

        if tap in ("kexp", "kraw"):
            for t in range(NT):
                sb.emit(GP, lambda e, t=t: e.dma_start(
                    out=Y[t * 128:(t + 1) * 128, :],
                    in_=kbuf[:, t * 1024:(t + 1) * 1024]),
                    reads=[("kT", m, t // TPS) for m in range(8)], dma=True)
            sb.finalize(GP)
            sb.finalize(SP)
            sb.replay(block, sems)
            st0.close()
            stA.close()
            return nc

        sb.barrier()
        st0.close()
        stB = contextlib.ExitStack()
        vbuf = sb_t(stB, "vbuf", [128, NT * D], BF16)
        SAVS = sb_t(stB, "SAVS", [128, 128], BF16)
        SAVRF = sb_t(stB, "SAVRF", [128, 128], BF16)
        pretouch(vbuf, "vbuf_pt", NT * D)
        pretouch(ctx, "ctx_pt", 8192)
        wvb = wkb
        sb.emit(GP, lambda e: e.dma_start(
            out=wvb[:].rearrange("p (c n) -> p c n", c=8),
            in_=WV.rearrange("(c p) n -> p c n", p=128)), writes=[("wv",)], dma=True, dsem="gw")
        for c in range(8):
            fns = [mm(psA[:, 3136 + c:3137 + c],
                      wvb[:, k * 1024 + c * 128:k * 1024 + c * 128 + 128],
                      scp_bf[:, k:k + 1], k == 0, k == 7) for k in range(8)]
            sb.emit(PE, fns, reads=[("wv",), ("scpbf",)], writes=PSB(6))
        sb.emit(VE, lambda e: e.tensor_copy(SAVS[:, 0:8], psA[:, 3136:3144]),
                reads=PSB(6), writes=[("savs",)])
        sb.emit(SP, lambda e: e.dma_start_transpose(SAVRF[:], SAVS[:]),
                reads=[("savs",)], writes=[("savr",)], dma=True)
        sb.emit(SP, lambda e: e.dma_start(
            out=ROWSD[0:1, 5 * D:6 * D].rearrange("o (r p) -> (o r) p", p=128),
            in_=SAVRF[0:8, :]), reads=[("savr",)], writes=[("savwd",)], dma=True)
        sb.emit(GP, lambda e: e.dma_start(
            out=savw_row[:], in_=ROWSD[0:1, 5 * D:6 * D]),
            reads=[("savwd",)], writes=[("savw",)], dma=True, dsem="gs")
        for k in range(8):
            sb.emit(VE, lambda e, k=k: e.tensor_scalar_mul(
                wvb[:, k * 1024:(k + 1) * 1024], wvb[:, k * 1024:(k + 1) * 1024],
                scp[:, 8 + k:9 + k]),
                reads=[("wv",), ("scp",)] + PSB(6), writes=[("wv",)])
        if tap == "wv":
            sb.emit(GP, lambda e: e.dma_start(
                out=Y[0:1024, :].rearrange("(c p) d -> p c d", p=128),
                in_=wvb[:].rearrange("p (c d) -> p c d", c=8)),
                reads=[("wv",)], dma=True)
            sb.finalize(GP)
            sb.finalize(SP)
            sb.replay(block, sems)
            stB.close()
            stA.close()
            return nc
        for sc in range(NSC):
            hts = sc % 2
            for i in range(TPS):
                t = sc * TPS + i
                xs, xbs = t % 2, t % 2
                xr = XRs[:, xs * D:(xs + 1) * D]
                emit_x_tile(t, xr, ("xr", xs))
                ln_stats_xhat(t, xr, ("xr", xs),
                              XB[:, xbs * D:(xbs + 1) * D], ("xb", xbs),
                              stat_rstd[:, 4 * t:4 * t + 1], stat_nmr[:, 4 * t:4 * t + 1],
                              ("rstd", t), ("nmr", t), False, TAx=TA2)
                emit_T(HT[:, hts * 4096:(hts + 1) * 4096], ("hT", hts, i),
                       XB[:, xbs * D:(xbs + 1) * D], ("xb", xbs), i * 128)
            if tap == "ht4" and sc == 1:
                sb.emit(GP, lambda e: e.dma_start(
                    out=Y[0:512, :].rearrange("(a p) d -> p a d", p=128),
                    in_=HT[:, 0:4096].rearrange("p (a d) -> p a d", a=4)),
                    reads=[("v", tt, nn) for tt in range(4) for nn in range(2)],
                    dma=True)
                sb.finalize(GP)
                sb.finalize(SP)
                sb.replay(block, sems)
                stB.close()
                stA.close()
                return nc
            if tap == "ht3" and sc == 0:
                sb.emit(GP, lambda e: e.dma_start(
                    out=Y[0:512, :].rearrange("(a p) d -> p a d", p=128),
                    in_=HT[:, 0:4096].rearrange("p (a d) -> p a d", a=4)),
                    reads=[("hT", 0, i) for i in range(TPS)], dma=True)
                sb.finalize(GP)
                sb.finalize(SP)
                sb.replay(block, sems)
                stB.close()
                stA.close()
                return nc
            for i in range(TPS):
                t = sc * TPS + i
                for n in range(2):
                    bk = (i * 2 + n) % 4
                    fns = [mm(bank(bk),
                              HT[:, hts * 4096 + c * 512 + i * 128:
                                 hts * 4096 + c * 512 + i * 128 + 128],
                              wvb[:, c * 1024 + n * 512:c * 1024 + (n + 1) * 512],
                              c == 0, False)
                           for c in range(8)]
                    fns.append(mm(bank(bk), ones_bf[:],
                                  savw_row[0:1, n * 512:(n + 1) * 512], False, True))
                    sb.emit(PE, fns,
                            reads=[("wv",), ("savw",), ("ones",), ("hT", hts, i)],
                            writes=PSB(bk))
                    sb.emit(SE, lambda e, t=t, n=n, bk=bk: e.activation(
                        vbuf[:, t * 1024 + n * 512:t * 1024 + (n + 1) * 512],
                        bank(bk), ACT.Identity),
                        reads=PSB(bk) + [("vbuf_pt",)], writes=[("v", t, n)])
        if tap == "v":
            for t in range(NT):
                sb.emit(GP, lambda e, t=t: e.dma_start(
                    out=Y[t * 128:(t + 1) * 128, :],
                    in_=vbuf[:, t * 1024:(t + 1) * 1024]),
                    reads=[("v", t, 0), ("v", t, 1)], dma=True)
            sb.finalize(GP)
            sb.finalize(SP)
            sb.replay(block, sems)
            stB.close()
            stA.close()
            return nc

        sb.barrier()
        ktm = sb_t(stB, "ktm", [128, 2 * D], BF16)
        for n in range(2):
            for t in range(NT):
                slot = t % 2
                sb.emit(SP, lambda e, t=t, s=slot: e.dma_start_transpose(
                    ktm[:, s * D:(s + 1) * D].rearrange("p (m i) -> p m i", m=8),
                    kbuf[:, t * 1024:(t + 1) * 1024].rearrange("p (m i) -> p m i", m=8)),
                    writes=[("ktm", slot)], dma=True)
                for m in range(8):
                    sb.emit(PE, [mm(bank(m),
                                    ktm[:, slot * D + m * 128:slot * D + (m + 1) * 128],
                                    vbuf[:, t * 1024 + n * 512:t * 1024 + (n + 1) * 512],
                                    t == 0, t == NT - 1)],
                            reads=[("ktm", slot)], writes=PSB(m))
            for m in range(8):
                sb.emit(SE, lambda e, m=m, n=n: e.activation(
                    ctx[:, m * 1024 + n * 512:m * 1024 + (n + 1) * 512],
                    bank(m), ACT.Identity, scale=rk[:, m:m + 1]),
                    reads=PSB(m) + [("rk",), ("ctx_pt",)], writes=[("ctx",)])
        if tap == "ctx":
            sb.emit(GP, lambda e: e.dma_start(
                out=Y.rearrange("(a p) d -> p a d", p=128)[:, 0:8, :],
                in_=ctx[:].rearrange("p (a d) -> p a d", a=8)),
                reads=[("ctx",)], dma=True)
            sb.finalize(GP)
            sb.finalize(SP)
            sb.replay(block, sems)
            stB.close()
            stA.close()
            return nc

        sb.barrier()
        stB.close()
        stA.close()
        stC = contextlib.ExitStack()
        wqb = sb_t(stC, "wqb", [128, 8 * D], BF16)
        wob = sb_t(stC, "wob", [128, 8 * D], BF16)
        qT = sb_t(stC, "qT", [128, 8192], BF16)
        aT = sb_t(stC, "aT", [128, 4096], BF16)
        oT = sb_t(stC, "oT", [128, 4096], BF16)
        hb4 = sb_t(stC, "hb4", [128, 6 * D], BF16)
        XN = sb_t(stC, "XN", [128, 2 * D], F32)
        XRe = sb_t(stC, "XRe", [128, 2 * D], F32)
        SCe = sb_t(stC, "SCe", [128, 2 * D], F32)
        rows4 = sb_t(stC, "rows4", [1, 4 * D], BF16)
        TA4 = sb_t(stC, "TA4", [128, 512], F32)
        TAQ = sb_t(stC, "TAQ", [128, 512], F32)
        otm = sb_t(stC, "otm", [128, 2 * D], BF16)
        mca1 = sb_t(stC, "mca1", [128, D], F32)
        msa = sb_t(stC, "msa", [128, D], F32)
        mgg = sb_t(stC, "mgg", [128, D], F32)
        mbg = sb_t(stC, "mbg", [128, D], F32)

        def xr4(t):
            s = t % 4
            if s < 2:
                return XRs[:, s * D:(s + 1) * D], ("xr", s)
            return XRe[:, (s - 2) * D:(s - 1) * D], ("xr", s)

        sb.emit(GP, lambda e: e.dma_start(
            out=wqb[:].rearrange("p (c n) -> p c n", c=8),
            in_=WQ.rearrange("(c p) n -> p c n", p=128)), writes=[("wq",)], dma=True, dsem="gw")
        sb.emit(GP, lambda e: e.dma_start(
            out=wob[:].rearrange("p (c n) -> p c n", c=8),
            in_=WO.rearrange("(c p) n -> p c n", p=128)), writes=[("wo",)], dma=True, dsem="gw")
        sb.emit(GP, lambda e: e.dma_start(out=rows4[:], in_=ROWSD[0:1, 0:4 * D]),
                writes=[("rows4",)], dma=True, dsem="gs")

        def bcast(row_ap, dst, key, rkey):
            for n in range(2):
                sb.emit(PE, [mm(bank(6 + n), ones_bf[:],
                                row_ap[0:1, n * 512:(n + 1) * 512], True, True)],
                        reads=[rkey, ("ones",)], writes=PSB(6 + n))
                sb.emit(VE, lambda e, n=n, d=dst: e.tensor_copy(
                    d[:, n * 512:(n + 1) * 512], bank(6 + n)),
                    reads=PSB(6 + n), writes=[key])

        bcast(rows4[0:1, 0:D], msa, ("msa",), ("rows4",))
        bcast(rows4[0:1, D:2 * D], mca1, ("mca1",), ("rows4",))
        bcast(rows4[0:1, 3 * D:4 * D], mgg, ("mgg",), ("rows4",))
        bcast(rows4[0:1, 2 * D:3 * D], mbg, ("mbg",), ("rows4",))

        s0 = SC0[:]
        s1 = SCe[:, 0:D]
        s2 = SCe[:, D:2 * D]
        for sc in range(NSC):
            hts, qs = sc % 2, sc % 2
            for i in range(TPS):
                t = sc * TPS + i
                hs, xbs = t % 6, t % 2
                xr, xrk = xr4(t)
                emit_x_tile(t, xr, xrk)
                ln_stats_xhat(t, xr, xrk, s0, ("sc0",),
                              stat_rstd[:, 4 * t:4 * t + 1], stat_nmr[:, 4 * t:4 * t + 1],
                              ("rstd", t), ("nmr", t), False, TAx=TA2)
                sb.emit(VE, lambda e: e.tensor_mul(s1, s0, mca1[:]),
                        reads=[("sc0",), ("mca1",)], writes=[("sc1",)])
                sb.emit(VE, lambda e, hs=hs: e.tensor_add(
                    hb4[:, hs * D:(hs + 1) * D], s1, msa[:]),
                    reads=[("sc1",), ("msa",)], writes=[("hb", hs)])
                emit_T(HT[:, hts * 4096:(hts + 1) * 4096], ("hT", hts, i),
                       hb4[:, hs * D:(hs + 1) * D], ("hb", hs), i * 128)
                for n in range(2):
                    bk = (i % 2) * 2 + n
                    fns = [mm(bank(bk),
                              HT[:, hts * 4096 + c * 512 + i * 128:
                                 hts * 4096 + c * 512 + i * 128 + 128],
                              wqb[:, c * 1024 + n * 512:c * 1024 + (n + 1) * 512],
                              c == 0, c == 7) for c in range(8)]
                    sb.emit(PE, fns, reads=[("wq",), ("hT", hts, i)], writes=PSB(bk))
                bk0 = (i % 2) * 2
                qps = psA[:, bk0 * 512:bk0 * 512 + 1024]
                qb = 16 * t
                qng = TAQ[:, qb:qb + 1]
                qsm = TAQ[:, qb + 4:qb + 5]
                rqv = TAQ[:, qb + 8:qb + 9]
                sb.emit(VE, lambda e, q=qps, qng=qng: e.reduce_max(
                    qng, q, axis=AX, negate=True),
                    reads=PSB(bk0, bk0 + 1), writes=[("qng", t)], slow=True)
                sb.emit(SE, [lambda e, q=qps, qng=qng, qsm=qsm: e.activation(
                    s2, q, ACT.Exp, bias=qng, accum_out=qsm),
                    lambda e: e.drain()],
                    reads=PSB(bk0, bk0 + 1) + [("qng", t)],
                    writes=[("sc2",), ("qsum", t)], slow=True)
                sb.emit(VE, lambda e, qsm=qsm, rqv=rqv: e.reciprocal(rqv, qsm),
                        reads=[("qsum", t)], writes=[("rq", t)])
                sb.emit(VE, lambda e, rqv=rqv, xbs=xbs: e.tensor_scalar(
                    XB[:, xbs * D:(xbs + 1) * D], s2,
                    rqv, 1.0 / 32.0, ALU.mult, ALU.mult),
                    reads=[("sc2",), ("rq", t)], writes=[("qe", xbs)])
                emit_T(qT[:, qs * 4096:(qs + 1) * 4096], ("qT", qs, i),
                       XB[:, xbs * D:(xbs + 1) * D], ("qe", xbs), i * 128)
            for me in range(8):
                bk = 4 + me % 2
                fns = [mm(bank(bk),
                          ctx[:, c * 1024 + me * 128:c * 1024 + me * 128 + 128],
                          qT[:, qs * 4096 + c * 512:qs * 4096 + (c + 1) * 512],
                          c == 0, c == 7) for c in range(8)]
                sb.emit(PE, fns,
                        reads=[("ctx",)] + [("qT", qs, i) for i in range(TPS)],
                        writes=PSB(bk))
                sb.emit(SE, lambda e, me=me, bk=bk: e.activation(
                    aT[:, me * 512:(me + 1) * 512], bank(bk), ACT.Identity),
                    reads=PSB(bk), writes=[("aT", me)])
            for md in range(8):
                bk = 6 + md % 2
                fns = [mm(bank(bk),
                          wob[:, ce * 1024 + md * 128:ce * 1024 + md * 128 + 128],
                          aT[:, ce * 512:(ce + 1) * 512],
                          ce == 0, ce == 7) for ce in range(8)]
                sb.emit(PE, fns, reads=[("wo",)] + [("aT", ce) for ce in range(8)],
                        writes=PSB(bk))
                sb.emit(SE, lambda e, md=md, bk=bk: e.activation(
                    oT[:, md * 512:(md + 1) * 512], bank(bk), ACT.Identity,
                    bias=bo_pc[:, md:md + 1]),
                    reads=PSB(bk) + [("bo_pc",)], writes=[("oT", md)])
            for i in range(TPS):
                t = sc * TPS + i
                hs, osl = t % 6, t % 2
                xr, xrk = xr4(t)
                for c in range(8):
                    sb.emit(SP, lambda e, o=i * 128, osl=osl, c=c:
                            e.dma_start_transpose(
                        otm[:, osl * D + c * 128:osl * D + (c + 1) * 128],
                        oT[:, c * 512 + o:c * 512 + o + 128]),
                        reads=[("oT", md) for md in range(8)],
                        writes=[("otm", osl)], dma=True)
                sb.emit(VE, lambda e, hs=hs, osl=osl: e.tensor_add(
                    s0, hb4[:, hs * D:(hs + 1) * D], otm[:, osl * D:(osl + 1) * D]),
                    reads=[("hb", hs), ("otm", osl)], writes=[("sc0",)])
                b = 16 * t
                a0 = TA4[:, b:b + 1]
                a1 = TA4[:, b + 4:b + 5]
                a2 = TA4[:, b + 8:b + 9]
                sd = TA4[:, b + 12:b + 13]
                k0, k1, k2, k3 = ("u", t, 0), ("u", t, 1), ("u", t, 2), ("u", t, 3)
                sb.emit(VE, lambda e, a0=a0: e.reduce_sum(a0, s0, axis=AX, negate=True),
                        reads=[("sc0",)], writes=[k0], slow=True)
                sb.emit(SE, [lambda e, a1=a1: e.activation(s2, s0, ACT.Square,
                                                           accum_out=a1),
                             lambda e: e.drain()],
                        reads=[("sc0",)], writes=[("sc2",), k1], slow=True)
                sb.emit(VE, lambda e, a0=a0: e.tensor_scalar_mul(a0, a0, 1.0 / D),
                        reads=[k0], writes=[k0])
                sb.emit(VE, lambda e, a1=a1: e.tensor_scalar_mul(a1, a1, 1.0 / D),
                        reads=[k1], writes=[k1])
                sb.emit(VE, lambda e, a0=a0, a2=a2: e.tensor_mul(a2, a0, a0),
                        reads=[k0], writes=[k2])
                sb.emit(VE, [lambda e, a1=a1, a2=a2: e.tensor_sub(a1, a1, a2),
                             lambda e: e.drain()],
                        reads=[k1, k2], writes=[k1])
                sb.emit(SE, lambda e, a1=a1, sd=sd: e.activation(
                    sd, a1, ACT.Sqrt, bias=epsb[:, 0:1]),
                    reads=[k1], writes=[k3])
                rr2 = TA4[:, b + 8:b + 9]
                sb.emit(VE, lambda e, sd=sd, rr2=rr2: e.reciprocal(rr2, sd),
                        reads=[k3, k2], writes=[("u", t, 4)])
                sb.emit(VE, [lambda e, a0=a0, rr2=rr2: e.tensor_mul(a0, a0, rr2),
                             lambda e: e.drain()],
                        reads=[k0, ("u", t, 4)], writes=[k0])
                sb.emit(SE, lambda e, a0=a0, rr2=rr2: e.activation(
                    s1, s0, ACT.Identity, bias=a0, scale=rr2),
                    reads=[("sc0",), k0, ("u", t, 4)], writes=[("sc1",)])
                xn = XN[:, (t % 2) * D:(t % 2 + 1) * D]
                sb.emit(VE, lambda e: e.tensor_mul(s2, s1, mgg[:]),
                        reads=[("sc1",), ("mgg",)], writes=[("sc2",)])
                sb.emit(VE, lambda e: e.tensor_add(s2, s2, mbg[:]),
                        reads=[("sc2",), ("mbg",)], writes=[("sc2",)])
                sb.emit(VE, lambda e, xn=xn, xr=xr: e.tensor_add(xn, s2, xr),
                        reads=[("sc2",), xrk], writes=[("xn", t % 2)])
                sb.emit(GP, lambda e, t=t, xn=xn: e.dma_start(
                    out=XNEW[t * 128:(t + 1) * 128, :], in_=xn),
                    reads=[("xn", t % 2)], writes=[("xnh", t)], dma=True)
        if tap == "xnew":
            for t in range(NT):
                xr, xrk = xr4(t)
                sb.emit(GP, lambda e, t=t, xr=xr: e.dma_start(
                    out=xr, in_=XNEW[t * 128:(t + 1) * 128, :]),
                    reads=[("xnh", t)], writes=[xrk], dma=True)
                sb.emit(SP, lambda e, t=t, xr=xr: e.dma_start(
                    out=Y[t * 128:(t + 1) * 128, :], in_=xr),
                    reads=[xrk], dma=True)
            sb.finalize(SP)
            sb.finalize(GP)
            sb.replay(block, sems)
            stC.close()
            return nc

        sb.barrier()
        stC.close()
        stD = contextlib.ExitStack()
        w1b = sb_t(stD, "w1b", [128, 8 * FF], BF16)
        w2b = sb_t(stD, "w2b", [128, 32 * D], BF16)
        fT = sb_t(stD, "fT", [128, 32 * 256], BF16)
        gbb = sb_t(stD, "gbb", [128, D], F32)
        rowg = sb_t(stD, "rowg", [1, D], BF16)
        TA5 = TA2
        st5r = sb_t(stD, "st5r", [128, 4 * NT], F32)
        st5n = sb_t(stD, "st5n", [128, 4 * NT], F32)
        otm5 = HT[:, 6144:8192]
        pretouch(fT, "fT_pt", 32 * 256)

        def xr5(t):
            s = t % 2
            return XRs[:, s * D:(s + 1) * D], ("xr", s)

        sb.emit(GP, lambda e: e.dma_start(
            out=w1b[:].rearrange("p (c n) -> p c n", c=8),
            in_=W1.rearrange("(c p) n -> p c n", p=128)), writes=[("w1",)], dma=True, dsem="gw")
        sb.emit(GP, lambda e: e.dma_start(
            out=w2b[:].rearrange("p (c n) -> p c n", c=32),
            in_=W2.rearrange("(c p) n -> p c n", p=128)), writes=[("w2",)], dma=True, dsem="gw")
        sb.emit(GP, lambda e: e.dma_start(out=rowg[:], in_=ROWSD[0:1, 4 * D:5 * D]),
                writes=[("rowg",)], dma=True, dsem="gs")
        for n in range(2):
            sb.emit(PE, [mm(bank(6 + n), ones_bf[:],
                            rowg[0:1, n * 512:(n + 1) * 512], True, True)],
                    reads=[("rowg",), ("ones",)], writes=PSB(6 + n))
            sb.emit(VE, lambda e, n=n: e.tensor_copy(
                gbb[:, n * 512:(n + 1) * 512], bank(6 + n)),
                reads=PSB(6 + n), writes=[("gbb",)])
        for mf in range(32):
            fns = [mm(psA[:, 3072 + mf:3073 + mf],
                      w1b[:, k * 4096 + mf * 128:k * 4096 + mf * 128 + 128],
                      scp_bf[:, 8 + k:9 + k], k == 0, k == 7) for k in range(8)]
            sb.emit(PE, fns, reads=[("w1",), ("scpbf",)], writes=PSB(6))
        sb.emit(VE, lambda e: e.tensor_add(b1f_pc[:], b1_pc[:], psA[:, 3072:3104]),
                reads=PSB(6) + [("b1_pc",)], writes=[("b1f",)])
        for k in range(8):
            sb.emit(VE, lambda e, k=k: e.tensor_scalar_mul(
                w1b[:, k * 4096:(k + 1) * 4096], w1b[:, k * 4096:(k + 1) * 4096],
                scp[:, 24 + k:25 + k]),
                reads=[("w1",), ("scp",)] + PSB(6), writes=[("w1",)])

        h2T = HT[:, 0:4096]
        o2T = HT[:, 4096:6144]
        for sc in range(NSC):
            for hh in range(2):
                for ii in range(2):
                    i = hh * 2 + ii
                    t = sc * TPS + i
                    xbs = t % 2
                    xr, xrk = xr5(t)
                    emit_x_tile(t, xr, xrk, src=XNEW)
                    ln_stats_xhat(t, xr, xrk,
                                  XB[:, xbs * D:(xbs + 1) * D], ("xb", xbs),
                                  st5r[:, 4 * t:4 * t + 1],
                                  st5n[:, 4 * t:4 * t + 1],
                                  ("r3", t), ("n3", t), True, TAx=TA5)
                    emit_T(h2T, ("h2T", i), XB[:, xbs * D:(xbs + 1) * D],
                           ("xb", xbs), i * 128)
                lo = hh * 256
                for mf in range(32):
                    bk = mf % 3
                    fns = [mm(bank(bk, 256),
                              w1b[:, c * 4096 + mf * 128:c * 4096 + mf * 128 + 128],
                              h2T[:, c * 512 + lo:c * 512 + lo + 256],
                              c == 0, c == 7) for c in range(8)]
                    sb.emit(PE, fns,
                            reads=[("w1",)] + [("h2T", i) for i in range(TPS)],
                            writes=PSB(bk))
                    sb.emit(SE, lambda e, mf=mf, bk=bk: e.activation(
                        fT[:, mf * 256:(mf + 1) * 256], bank(bk, 256), ACT.Gelu,
                        bias=b1f_pc[:, mf:mf + 1]),
                        reads=PSB(bk) + [("b1f",), ("fT_pt",)], writes=[("fT", mf)])
                for md in range(8):
                    bk = 3 + md % 3
                    fns = [mm(bank(bk, 256),
                              w2b[:, mf * 1024 + md * 128:mf * 1024 + md * 128 + 128],
                              fT[:, mf * 256:(mf + 1) * 256],
                              mf == 0, mf == 31) for mf in range(32)]
                    sb.emit(PE, fns,
                            reads=[("w2",)] + [("fT", mf) for mf in range(32)],
                            writes=PSB(bk))
                    sb.emit(SE, lambda e, md=md, bk=bk: e.activation(
                        o2T[:, md * 256:(md + 1) * 256], bank(bk, 256), ACT.Identity,
                        bias=b2_pc[:, md:md + 1]),
                        reads=PSB(bk) + [("b2_pc",)], writes=[("o2T", md)])
                for ii in range(2):
                    i = hh * 2 + ii
                    t = sc * TPS + i
                    osl = t % 2
                    xr, xrk = xr5(t)
                    for c in range(8):
                        sb.emit(SP, lambda e, o=ii * 128, osl=osl, c=c:
                                e.dma_start_transpose(
                            HT[:, 6144 + osl * D + c * 128:
                               6144 + osl * D + (c + 1) * 128],
                            o2T[:, c * 256 + o:c * 256 + o + 128]),
                            reads=[("o2T", md) for md in range(8)],
                            writes=[("otm", osl)], dma=True)
                    sb.emit(VE, lambda e, osl=osl: e.tensor_mul(
                        SC0[:], otm5[:, osl * D:(osl + 1) * D], gbb[:]),
                        reads=[("otm", osl), ("gbb",)], writes=[("sc0",)])
                    sb.emit(VE, lambda e, xr=xr: e.tensor_add(SC0[:], SC0[:], xr),
                            reads=[("sc0",), xrk], writes=[("sc0",)])
                    sb.emit(SP, lambda e, t=t: e.dma_start(
                        out=Y[t * 128:(t + 1) * 128, :], in_=SC0[:]),
                        reads=[("sc0",)], dma=True)

        sb.finalize(SP)
        sb.finalize(GP)
        sb.replay(block, sems)
        stD.close()
    return nc


_NC_CACHE = {}


def _wmod_pc(wmod):
    idx = []
    for j in (0, 1, 3, 4):
        base = j * 1024
        for p in range(128):
            for c in range(8):
                idx.append(base + c * 128 + p)
    return np.ascontiguousarray(wmod[:, idx])


def kernel(L0_attn_x, L0_attn_Wq, L0_attn_Wk, L0_attn_Wv, L0_attn_Wo, L0_attn_bo,
           L0_attn_gamma, L0_attn_beta, L0_ffn_w1, L0_ffn_b1, L0_ffn_w2, L0_ffn_b2,
           L0_c, L0_wmod):
    x = np.ascontiguousarray(np.asarray(L0_attn_x, dtype=np.float32))
    common = dict(
        wq_in=np.ascontiguousarray(L0_attn_Wq, np.float32),
        wk_in=np.ascontiguousarray(L0_attn_Wk, np.float32),
        wv_in=np.ascontiguousarray(L0_attn_Wv, np.float32),
        wo_in=np.ascontiguousarray(L0_attn_Wo, np.float32),
        bo_in=np.ascontiguousarray(L0_attn_bo, np.float32),
        gamma_in=np.ascontiguousarray(L0_attn_gamma, np.float32),
        beta_in=np.ascontiguousarray(L0_attn_beta, np.float32),
        w1_in=np.ascontiguousarray(L0_ffn_w1, np.float32),
        b1_in=np.ascontiguousarray(L0_ffn_b1, np.float32),
        w2_in=np.ascontiguousarray(L0_ffn_w2, np.float32),
        b2_in=np.ascontiguousarray(L0_ffn_b2, np.float32),
        wmod_in=np.ascontiguousarray(L0_wmod, np.float32),
        wmodpc_in=_wmod_pc(np.asarray(L0_wmod, np.float32)),
        ident_in=np.eye(128, dtype=np.float32),
    )
    if "nc" not in _NC_CACHE:
        _NC_CACHE["nc"] = build()
    nc = _NC_CACHE["nc"]
    in_maps = []
    for b in range(8):
        m = dict(common)
        m["x"] = np.ascontiguousarray(x[b])
        m["c_in"] = np.ascontiguousarray(np.asarray(L0_c, np.float32)[b])
        in_maps.append(m)
    res = run_bass_kernel_spmd(nc, in_maps, core_ids=list(range(8)))
    return np.stack([r["y"] for r in res.results], axis=0)
